# version 17
# baseline (speedup 1.0000x reference)
import numpy as np
import ml_dtypes
import contextlib
import concourse.bass as bass
import concourse.mybir as mybir
from concourse.bass_utils import run_bass_kernel_spmd

F32 = mybir.dt.float32
BF16 = mybir.dt.bfloat16
F8 = mybir.dt.float8e4
PM = mybir.MatmulPerfMode
AF = mybir.ActivationFunctionType
ALU = mybir.AluOpType
AX = mybir.AxisListType

T = 1024
NT = 8
NK = 16
NH = 16
NG = T // 8
NB = NG // 4
CSC = 1.0 / np.sqrt(128.0)

ENGS = ("sync", "gpsimd", "vector", "scalar", "tensor")


class Sem:
    def __init__(self, h):
        self.h = h
        self.n = 0

    def inc(self, k=1):
        self.n += k
        return self.n


class _Cut(Exception):
    pass


def build_program(ncut=4):
    nc = bass.Bass("TRN2", target_bir_lowering=False)

    x_in = nc.dram_tensor("x", [T, 2048], F32, kind="ExternalInput").ap()
    wqkv = nc.dram_tensor("wqkv", [2048, 6144], F32, kind="ExternalInput").ap()
    wout = nc.dram_tensor("wout", [2048, 2048], F32, kind="ExternalInput").ap()
    bout = nc.dram_tensor("bout", [1, 2048], F32, kind="ExternalInput").ap()
    in2 = nc.dram_tensor("in2", [2048, 2048], F32, kind="ExternalInput").ap()
    sc = nc.dram_tensor("sc", [1, 1], F32, kind="ExternalInput").ap()
    sc1 = nc.dram_tensor("sc1", [1, 1], F32, kind="ExternalInput").ap()
    id32 = nc.dram_tensor("id32", [128, 128], F32, kind="ExternalInput").ap()
    idbf = nc.dram_tensor("idbf", [128, 128], BF16, kind="ExternalInput").ap()
    maskb = nc.dram_tensor("maskb", [128, 128], BF16, kind="ExternalInput").ap()
    y_out = nc.dram_tensor("y", [T, 2048], F32, kind="ExternalOutput").ap()

    ops = {e: [] for e in ENGS}

    def emit(eng, fn):
        ops[eng].append(fn)

    es = contextlib.ExitStack()
    with es:
        ec = es.enter_context
        AB = ec(nc.sbuf_tensor("AB", [128, 16384], BF16))
        QT = ec(nc.sbuf_tensor("QT", [128, 16384], F8))
        KV1 = ec(nc.sbuf_tensor("KV1", [128, 16384], F8))
        VD = ec(nc.sbuf_tensor("VD", [128, 16384], BF16))
        Q1T = ec(nc.sbuf_tensor("Q1T", [128, 16384], F8))
        SCR32 = ec(nc.sbuf_tensor("SCR32", [128, 8192], F32))
        SCRBF = ec(nc.sbuf_tensor("SCRBF", [128, 4096], BF16))
        id32t = ec(nc.sbuf_tensor("id32t", [128, 128], F32))
        idbft = ec(nc.sbuf_tensor("idbft", [128, 128], BF16))
        maskt = ec(nc.sbuf_tensor("maskt", [128, 128], BF16))
        onesc = ec(nc.sbuf_tensor("onesc", [128, 1], BF16))
        ones1r = ec(nc.sbuf_tensor("ones1r", [1, 128], BF16))
        scol = ec(nc.sbuf_tensor("scol", [128, 1], F32))
        s1col = ec(nc.sbuf_tensor("s1col", [128, 1], F32))
        ccol = ec(nc.sbuf_tensor("ccol", [128, 1], F32))
        ctmp = ec(nc.sbuf_tensor("ctmp", [128, 2], F32))
        stat = ec(nc.sbuf_tensor("stat", [128, 32], F32))
        rscf = ec(nc.sbuf_tensor("rscf", [128, 8], F32))
        rsbf = ec(nc.sbuf_tensor("rsbf", [128, 8], BF16))
        brt = ec(nc.sbuf_tensor("brt", [1, 2048], BF16))
        W8 = ec(nc.sbuf_tensor("W8", [128, 8192], F8))
        WST = ec(nc.sbuf_tensor("WST", [128, 4096], F32))

        PT = [ec(nc.psum_tensor(f"PT{i}", [128, 512], F32)) for i in range(6)]
        PVs = [ec(nc.psum_tensor("PV0", [128, 1024], BF16)),
               ec(nc.psum_tensor("PV1", [128, 1024], BF16))]

        sC = Sem(ec(nc.semaphore("sC")))
        sX = [Sem(ec(nc.semaphore(f"sX{i}"))) for i in range(4)]
        sIN = [Sem(ec(nc.semaphore(f"sIN{i}"))) for i in range(8)]
        sXR = [Sem(ec(nc.semaphore(f"sXR{i}"))) for i in range(2)]
        sST = [Sem(ec(nc.semaphore(f"sST{i}"))) for i in range(4)]
        sWQ = [[Sem(ec(nc.semaphore(f"sWQ{i}{h}"))) for h in range(2)] for i in range(2)]
        sWO = [Sem(ec(nc.semaphore(f"sWO{i}"))) for i in range(2)]
        sBR = Sem(ec(nc.semaphore("sBR")))
        sT = Sem(ec(nc.semaphore("sT")))
        sV = Sem(ec(nc.semaphore("sV")))
        sS = Sem(ec(nc.semaphore("sS")))
        sG = Sem(ec(nc.semaphore("sG")))
        ESEM = {"vector": sV, "scalar": sS, "tensor": sT, "gpsimd": sG}

        _cut_box = []

        last_w = {}
        readers = {}
        seen = {e: {} for e in ENGS}

        def _wait(eng, sem_, val):
            if val <= 0 or seen[eng].get(sem_, 0) >= val:
                return
            seen[eng][sem_] = val
            emit(eng, lambda e, s=sem_.h, v=val: e.wait_ge(s, v))

        def op(eng, fn, reads=(), writes=(), q=None):
            deps = {}
            for k in reads:
                w = last_w.get(k)
                if w:
                    deps[w[0]] = max(deps.get(w[0], 0), w[1])
            for k in writes:
                w = last_w.get(k)
                if w:
                    deps[w[0]] = max(deps.get(w[0], 0), w[1])
                for s_, v in readers.get(k, {}).items():
                    deps[s_] = max(deps.get(s_, 0), v)
            for s_, v in deps.items():
                _wait(eng, s_, v)
            sem_ = q if q is not None else ESEM[eng]
            amt = 16 if q is not None else 1
            emit(eng, lambda e, fn=fn, s=sem_.h, a=amt: fn(e).then_inc(s, a))
            t = sem_.inc(amt)
            tok = (sem_, t)
            for k in reads:
                readers.setdefault(k, {})[sem_] = t
            for k in writes:
                last_w[k] = tok
                readers[k] = {}
            return tok

        XT_K = [[("s32", 8), ("s32", 9), ("s32", 10)], [("s32", 11)],
                [("VDA",)], [("VDA",)]]
        XN_K = [[("s32", 0), ("s32", 1), ("s32", 2), ("s32", 3)],
                [("s32", 4), ("s32", 5), ("s32", 6), ("s32", 7)],
                [("K1A",)], [("K1B",)]]
        xt_sl = [SCR32.ap()[:, 4096:6144], SCR32.ap()[:, 6144:8192],
                 VD.ap()[:, 0:4096].bitcast(F32), VD.ap()[:, 4096:8192].bitcast(F32)]
        xn_sl = [SCR32.ap()[:, 0:2048], SCR32.ap()[:, 2048:4096],
                 KV1.ap()[:, 0:8192].bitcast(F32), KV1.ap()[:, 8192:16384].bitcast(F32)]
        BF_K = [("bf", i) for i in range(8)]

        def _phases():
         op("sync", lambda e: e.dma_start(out=id32t[:], in_=id32), writes=[("id32",)], q=sC)
         op("sync", lambda e: e.dma_start(out=idbft[:], in_=idbf), writes=[("idbf",)], q=sC)
         op("sync", lambda e: e.dma_start(out=maskt[:], in_=maskb), writes=[("maskt",)], q=sC)
         op("sync", lambda e: e.dma_start(out=scol[:], in_=sc.partition_broadcast(128)),
            writes=[("scol",)], q=sC)
         op("sync", lambda e: e.dma_start(out=s1col[:], in_=sc1.partition_broadcast(128)),
            writes=[("s1col",)], q=sC)
         for _k in (("id32",), ("idbf",), ("maskt",), ("scol",), ("s1col",)):
             last_w[_k] = (sC, sC.n)
         op("vector", lambda e: e.memset(onesc[:], 1.0), writes=[("onesc",)])
         op("vector", lambda e: e.memset(ones1r[:], 1.0), writes=[("ones1r",)])
         op("vector", lambda e: e.reciprocal(ctmp[:, 0:1], scol[:]),
            reads=[("scol",)], writes=[("ctmp",)])
         op("vector", lambda e: e.reciprocal(ctmp[:, 1:2], s1col[:]),
            reads=[("s1col",), ("ctmp",)], writes=[("ctmp",)])
         op("vector", lambda e: e.tensor_tensor(ccol[:], ctmp[:, 0:1], ctmp[:, 1:2], ALU.mult),
            reads=[("ctmp",)], writes=[("ccol",)])
         op("gpsimd", lambda e: e.dma_start(out=brt.ap()[0:1, :], in_=bout),
            writes=[("brt",)], q=sBR)

         def p1_load_stats(tt):
             p = tt % 4
             op("sync", lambda e, p=p, tt=tt: e.dma_start(
                 out=xt_sl[p], in_=x_in[tt * 128:(tt + 1) * 128, :]),
                writes=XT_K[p], q=sX[p])

             st = stat.ap()[:, 8 * p:8 * p + 8]
             msum, m_, v1, m2, varep, r1, rstd, ssq = [st[:, i:i + 1] for i in range(8)]
             SK = [("stat", p)]

             op("vector", lambda e, p=p, msum=msum: e.reduce_sum(msum, xt_sl[p], AX.X),
                reads=XT_K[p], writes=SK)
             op("vector", lambda e, msum=msum, m_=m_: e.tensor_scalar(
                 m_, msum, 1.0 / 2048.0, None, ALU.mult), reads=SK, writes=SK)
             op("scalar", lambda e, p=p, ssq=ssq: e.activation(
                 xn_sl[p], xt_sl[p], AF.Square, accum_out=ssq),
                reads=XT_K[p], writes=XN_K[p] + SK)
             op("vector", lambda e, v1=v1, ssq=ssq: e.tensor_scalar(
                 v1, ssq, 1.0 / 2048.0, None, ALU.mult), reads=SK, writes=SK)
             op("vector", lambda e, m_=m_, m2=m2: e.tensor_tensor(m2, m_, m_, ALU.mult),
                reads=SK, writes=SK)
             op("vector", lambda e, v1=v1, m2=m2, varep=varep: e.tensor_scalar(
                 varep, v1, m2, 1e-5, ALU.subtract, ALU.add), reads=SK, writes=SK)
             op("vector", lambda e, varep=varep, r1=r1: e.reciprocal(r1, varep),
                reads=SK, writes=SK)
             op("scalar", lambda e, r1=r1, rstd=rstd: e.activation(rstd, r1, AF.Sqrt),
                reads=SK, writes=SK)

         def p1_xn_transpose(tt):
             p = tt % 4
             st = stat.ap()[:, 8 * p:8 * p + 8]
             m_, rstd = st[:, 1:2], st[:, 6:7]
             SK = [("stat", p)]
             op("vector", lambda e, p=p, m_=m_, rstd=rstd: e.tensor_scalar(
                 xn_sl[p], xt_sl[p], m_, rstd, ALU.subtract, ALU.mult),
                reads=XT_K[p] + SK, writes=XN_K[p])

             for j in range(4):
                 bank = PT[(tt * 4 + j) % 3]
                 bk = ("PT", (tt * 4 + j) % 3)

                 def trgroup(e, bank=bank, p=p, j=j):
                     inst = None
                     for i in range(4):
                         k = 4 * j + i
                         inst = e.transpose(bank.ap()[:, i * 128:(i + 1) * 128],
                                            xn_sl[p][:, k * 128:(k + 1) * 128], id32t[:])
                     return inst
                 op("tensor", trgroup, reads=XN_K[p] + [("id32",)], writes=[bk])

                 dst_xn8 = AB.ap().bitcast(F8)[:, 0:16384].rearrange(
                     "q (k t) -> q k t", k=16)[:, 4 * j:4 * j + 4,
                                               tt * 128:(tt + 1) * 128]
                 src3 = bank.ap().rearrange("q (k t) -> q k t", k=4)
                 op("scalar", lambda e, dst=dst_xn8, src=src3: e.copy(dst, src),
                    reads=[bk], writes=[("AB",), ("rr",) + bk])

                 dst_q1 = Q1T.ap().rearrange("q (k t) -> q k t", k=16)[
                     :, 4 * j:4 * j + 4, tt * 128:(tt + 1) * 128]
                 op("vector", lambda e, dst=dst_q1, bank=bank: e.tensor_scalar(
                     dst, bank.ap().rearrange("q (k t) -> q k t", k=4), scol[:],
                     None, ALU.mult),
                    reads=[bk, ("scol",), ("rr",) + bk], writes=[("Q1T",)])

         for tt in range(4):
             p1_load_stats(tt)
         for tt in range(4):
             p1_xn_transpose(tt)
             p1_load_stats(tt + 4)
         for tt in range(4, NT):
             p1_xn_transpose(tt)

         if ncut < 2:
             raise _Cut
         xnT3 = AB.ap().rearrange("q (k t) -> q k t", k=16)
         kv_view = {
             "k": KV1.ap().rearrange("q (g s H) -> q g s H", s=8, H=16),
             "v": VD.ap().rearrange("q (g s H) -> q g s H", s=8, H=16),
         }
         KV_KEYS = {("k", 0): ("K1A",), ("k", 1): ("K1B",),
                    ("v", 0): ("VDA",), ("v", 1): ("VDB",)}

         xn8v = AB.ap().bitcast(F8)[:, 0:16384].rearrange("q (k t) -> q k t", k=16)
         w8sl = [W8.ap()[:, 2048 * i:2048 * (i + 1)] for i in range(4)]

         def emit_wdma(ft):
             for h2 in range(2):
                 op("sync", lambda e, ft=ft, h2=h2: e.dma_start(
                     out=WST.ap()[:, (ft % 2) * 2048 + h2 * 1024:
                                  (ft % 2) * 2048 + (h2 + 1) * 1024]
                         .rearrange("p (k f) -> p k f", f=128),
                     in_=wqkv.rearrange("(k p) (ft f) -> p k ft f", p=128, f=128)
                         [:, 8 * h2:8 * h2 + 8, ft, :]),
                    writes=[("wst", ft % 2, h2)], q=sWQ[ft % 2][h2])
                 op("vector", lambda e, ft=ft, h2=h2: e.tensor_scalar(
                     w8sl[ft % 4][:, h2 * 1024:(h2 + 1) * 1024],
                     WST.ap()[:, (ft % 2) * 2048 + h2 * 1024:
                              (ft % 2) * 2048 + (h2 + 1) * 1024], 16.0, None, ALU.mult),
                    reads=[("wst", ft % 2, h2)], writes=[("w8", ft % 4)])

         for _ft in range(4):
             emit_wdma(_ft)
         for ft in range(48):
             bkA = ("PT", (2 * ft) % 6)
             bkB = ("PT", (2 * ft + 1) % 6)
             bankA = PT[(2 * ft) % 6]
             bankB = PT[(2 * ft + 1) % 6]

             def mmgroup(e, bankA=bankA, bankB=bankB, ft=ft):
                 inst = None
                 w83 = w8sl[ft % 4].rearrange("p (k f) -> p k f", f=128)
                 for j in range(NK // 2):
                     for half, bank in ((0, bankA), (1, bankB)):
                         inst = e.matmul(
                             bank.ap()[:, 0:512],
                             w83[:, 2 * j:2 * j + 2, :],
                             xn8v[:, 2 * j:2 * j + 2, half * 512:(half + 1) * 512],
                             start=(j == 0), stop=(j == NK // 2 - 1),
                             perf_mode=PM.DoubleRow)
                 return inst
             op("tensor", mmgroup, reads=[("w8", ft % 4), ("AB",)], writes=[bkA, bkB])

             for half, bank, bk in ((0, bankA, bkA), (1, bankB, bkB)):
                 if ft < 16:
                     h = ft
                     op("scalar", lambda e, h=h, half=half, bank=bank: e.mul(
                         QT.ap()[:, h * 1024 + half * 512: h * 1024 + (half + 1) * 512],
                         bank.ap()[:, 0:512], 0.0625), reads=[bk], writes=[("QT",)])
                 else:
                     H = (ft - 16) % 16
                     which = "k" if ft < 32 else "v"
                     dst = kv_view[which][:, half * 64:(half + 1) * 64, :, H]
                     op("scalar", lambda e, dst=dst, bank=bank: e.mul(
                         dst, bank.ap().rearrange("q (g s) -> q g s", s=8), 0.0625),
                        reads=[bk], writes=[KV_KEYS[(which, half)]])
             if ft + 4 < 48:
                 emit_wdma(ft + 4)

         if ncut < 3:
             raise _Cut
         wm_sl = [SCRBF.ap()[:, 0:512], SCRBF.ap()[:, 512:1024]]
         rb_sl = [SCRBF.ap()[:, 1024:1536], SCRBF.ap()[:, 1536:2048]]
         vp_sl = [SCRBF.ap()[:, 2048:2560], SCRBF.ap()[:, 2560:3072]]
         wraw_sl = [SCRBF.ap()[:, 3072:3584], SCRBF.ap()[:, 3584:4096]]
         WM_K = [BF_K[0], BF_K[1]]
         RB_K = [BF_K[2], BF_K[3]]
         VP_K = [BF_K[4], BF_K[5]]
         WR_K = [BF_K[6], BF_K[7]]
         qt3 = QT.ap().rearrange("q (h t) -> q t h", h=16)
         atT4 = AB.ap().bitcast(F8)[:, 0:16384].rearrange("q (h g s) -> q h g s", g=128, s=8)
         mask_b = maskt.ap().unsqueeze(1).broadcast_to((128, 4, 128))

         def kv_key_of_group(base, g):
             return (base + ("A" if g < 64 else "B"),)

         def p3_tensor_front(b):
             pb = b % 2

             def vpgroup(e, b=b, pb=pb):
                 inst = None
                 for i in range(4):
                     g = 4 * b + i
                     inst = e.transpose(PVs[pb].ap()[:, i * 128:(i + 1) * 128],
                                        VD.ap()[:, g * 128:(g + 1) * 128], idbft[:])
                 return inst
             vk = [kv_key_of_group("VD", 4 * b), ("idbf",)]
             op("tensor", vpgroup, reads=vk, writes=[("PV", pb)])

             def scgroup(e, b=b, pb=pb):
                 inst = None
                 for i in range(4):
                     g = 4 * b + i
                     inst = e.matmul(PT[pb].ap()[:, i * 128:(i + 1) * 128],
                                     KV1.ap()[:, g * 128:(g + 1) * 128],
                                     qt3[:, g * 8:(g + 1) * 8, :],
                                     start=True, stop=True)
                 return inst
             op("tensor", scgroup, reads=[kv_key_of_group("K1", 4 * b), ("QT",)],
                writes=[("PT", pb)])

         def p3_expmask(b):
             pb = b % 2
             op("scalar", lambda e, pb=pb: e.activation(
                 wraw_sl[pb], PT[pb].ap()[:, 0:512], AF.Exp, scale=CSC),
                reads=[("PT", pb)], writes=[WR_K[pb]])
             op("gpsimd", lambda e, pb=pb: e.tensor_tensor(
                 wm_sl[pb].rearrange("q (i m) -> q i m", i=4),
                 wraw_sl[pb].rearrange("q (i m) -> q i m", i=4),
                 mask_b, ALU.mult),
                reads=[WR_K[pb], ("maskt",)], writes=[WM_K[pb]])

         def p3_consume(b):
             pb = b % 2

             def sggroup(e, pb=pb):
                 inst = None
                 for i in range(4):
                     inst = e.matmul(PT[2 + pb].ap()[:, 508 + i: 509 + i],
                                     wm_sl[pb][:, i * 128:(i + 1) * 128], onesc[:],
                                     start=True, stop=True)
                 return inst
             op("tensor", sggroup, reads=[WM_K[pb], ("onesc",)], writes=[("PT", 2 + pb)])

             op("vector", lambda e, b=b, pb=pb: e.reciprocal(
                 rscf.ap()[:, 4 * (b % 2):4 * (b % 2) + 4],
                 PT[2 + pb].ap()[:, 508:512]),
                reads=[("PT", 2 + pb)], writes=[("rs", b % 2)])
             op("vector", lambda e, b=b: e.tensor_copy(
                 rsbf.ap()[:, 4 * (b % 2):4 * (b % 2) + 4],
                 rscf.ap()[:, 4 * (b % 2):4 * (b % 2) + 4]),
                reads=[("rs", b % 2)], writes=[("rsb", b % 2)])

             def rbgroup(e, b=b, pb=pb):
                 inst = None
                 for i in range(4):
                     inst = e.matmul(
                         PT[2 + pb].ap()[:, i * 128:(i + 1) * 128],
                         rsbf.ap()[:, 4 * (b % 2) + i: 4 * (b % 2) + i + 1]
                             .broadcast_to((128, 128)),
                         idbft[:], start=True, stop=True)
                 return inst
             op("tensor", rbgroup, reads=[("rsb", b % 2), ("idbf",)], writes=[("PT", 2 + pb)])

             op("vector", lambda e, pb=pb: e.tensor_copy(rb_sl[pb], PT[2 + pb].ap()[:, 0:512]),
                reads=[("PT", 2 + pb)], writes=[RB_K[pb]])

             op("vector", lambda e, pb=pb: e.tensor_copy(
                 vp_sl[pb], PVs[pb].ap()[:, 0:512]),
                reads=[("PV", pb)], writes=[VP_K[pb]])

             def atgroup(e, pb=pb):
                 inst = None
                 for i in range(4):
                     inst = e.matmul(PT[4 + pb].ap()[:, i * 128:(i + 1) * 128],
                                     vp_sl[pb][:, i * 128:(i + 1) * 128],
                                     wm_sl[pb][:, i * 128:(i + 1) * 128],
                                     start=True, stop=True)
                 return inst
             op("tensor", atgroup, reads=[VP_K[pb], WM_K[pb]], writes=[("PT", 4 + pb)])

             op("vector", lambda e, b=b, pb=pb: e.tensor_tensor(
                 atT4[:, :, 4 * b:4 * b + 4, :],
                 PT[4 + pb].ap().rearrange("q (g s h) -> q h g s", g=4, s=8),
                 rb_sl[pb].rearrange("q (g s h) -> q h g s", g=4, s=8),
                 ALU.mult),
                reads=[("PT", 4 + pb), RB_K[pb]], writes=[("AB",)])

         w4s = [KV1.ap()[:, 0:8192], KV1.ap()[:, 8192:16384]]
         q2s = [VD.ap()[:, 0:8192].bitcast(F8)[:, 0:8192],
                VD.ap()[:, 8192:16384].bitcast(F8)[:, 0:8192]]
         W4K = [("K1A",), ("K1B",)]
         Q2K = [("VDA",), ("VDB",)]

         w48 = [KV1.ap()[:, 0:8192], KV1.ap()[:, 8192:16384]]

         def emit_wout(ncn):
             ws = ncn % 2
             for h2 in range(2):
                 op("gpsimd", lambda e, ncn=ncn, h2=h2, ws=ws: e.dma_start(
                     out=w48[ws][:, h2 * 4096:(h2 + 1) * 4096]
                         .rearrange("p (k f) -> p k f", f=512),
                     in_=wout.rearrange("(k p) (n f) -> p k n f", p=128, f=512)
                         [:, 8 * h2:8 * h2 + 8, ncn, :]),
                    writes=[W4K[ws]], q=sWO[ws])

         WSTK = [("wst", 0, 0), ("wst", 0, 1), ("wst", 1, 0), ("wst", 1, 1)]

         def emit_in2q(ncn, eng, nslots):
             ws = ncn % 2
             for half in range(2):
                 op("sync", lambda e, ncn=ncn, half=half: e.dma_start(
                     out=WST.ap().rearrange("p (k f) -> p k f", f=512),
                     in_=in2.rearrange("(k p) (n f) -> p k n f", p=128, f=512)
                         [:, 8 * half:8 * half + 8, ncn, :]),
                    writes=WSTK, q=sIN[half])
                 op("scalar", lambda e, half=half, ws=ws: e.activation(
                     q2s[ws][:, half * 4096:(half + 1) * 4096], WST.ap()[:],
                     AF.Copy, scale=s1col[:]),
                    reads=WSTK + [("s1col",)], writes=[Q2K[ws]])

         for b in range(NB):
             p3_tensor_front(b)
             p3_expmask(b)
             if b == 16:
                 emit_wout(0)
                 emit_in2q(0, "vector", 8)
             if b > 0:
                 p3_consume(b - 1)
         p3_consume(NB - 1)
         emit_wout(1)

         if ncut < 4:
             raise _Cut
         b_row = brt.ap()[0:1, :]
         xre_sl = [SCR32.ap()[:, 5120:5632], SCR32.ap()[:, 6144:6656]]
         XRE_K = [("s32", 10), ("s32", 11)]
         out_sl = [SCR32.ap()[:, 2048 + i * 512:2048 + (i + 1) * 512] for i in range(4)]
         OUT_K = [("s32", 4), ("s32", 5), ("s32", 6), ("s32", 7)]
         tmp_sl = [SCR32.ap()[:, 4096:4608], SCR32.ap()[:, 4608:5120]]
         TMP_K = [("s32", 8), ("s32", 9)]

         at83 = AB.ap().bitcast(F8)[:, 0:16384].rearrange("q (k t) -> q k t", k=16)
         q13 = Q1T.ap().rearrange("q (k t) -> q k t", k=16)

         for ncn in range(4):
             ws = ncn % 2
             if ncn + 1 < 4:
                 emit_in2q(ncn + 1, "scalar", 4)

             for tt in range(NT):
                 j = ncn * NT + tt
                 pj = j % 2
                 bA = PT[2 * pj]
                 bF = PT[1 + 2 * pj]
                 bAk = ("PT", 2 * pj)
                 bFk = ("PT", 1 + 2 * pj)

                 op("gpsimd", lambda e, tt=tt, ncn=ncn, pj=pj: e.dma_start(
                     out=xre_sl[pj],
                     in_=x_in[tt * 128:(tt + 1) * 128, ncn * 512:(ncn + 1) * 512]),
                    writes=[XRE_K[pj]], q=sXR[pj])

                 def agroup(e, bA=bA, ncn=ncn, tt=tt, ws=ws):
                     e.matmul(bA.ap()[:, 0:512], ones1r[:],
                              b_row[:, ncn * 512:(ncn + 1) * 512], start=True, stop=False)
                     inst = None
                     w483 = w48[ws].rearrange("p (k n) -> p k n", n=512)
                     for j in range(NH // 2):
                         inst = e.matmul(bA.ap()[:, 0:512],
                                         at83[:, 2 * j:2 * j + 2, tt * 128:(tt + 1) * 128],
                                         w483[:, 2 * j:2 * j + 2, :],
                                         start=False, stop=(j == NH // 2 - 1),
                                         perf_mode=PM.DoubleRow)
                     return inst
                 op("tensor", agroup, reads=[("ones1r",), ("brt",), ("AB",), W4K[ws]],
                    writes=[bAk])

                 def fgroup(e, bF=bF, tt=tt, ws=ws):
                     inst = None
                     q23 = q2s[ws].rearrange("p (k n) -> p k n", n=512)
                     for j in range(NK // 2):
                         inst = e.matmul(bF.ap()[:, 0:512],
                                         q13[:, 2 * j:2 * j + 2, tt * 128:(tt + 1) * 128],
                                         q23[:, 2 * j:2 * j + 2, :],
                                         start=(j == 0), stop=(j == NK // 2 - 1),
                                         perf_mode=PM.DoubleRow)
                     return inst
                 op("tensor", fgroup, reads=[("Q1T",), Q2K[ws]], writes=[bFk])

                 osl = out_sl[j % 4]
                 op("vector", lambda e, pj=pj, bF=bF: e.scalar_tensor_tensor(
                     tmp_sl[pj], bF.ap()[:, 0:512], ccol[:], xre_sl[pj],
                     ALU.mult, ALU.add),
                    reads=[bFk, ("ccol",), XRE_K[pj]], writes=[TMP_K[pj]])
                 op("vector", lambda e, osl=osl, pj=pj, bA=bA: e.tensor_tensor(
                     osl, tmp_sl[pj], bA.ap()[:, 0:512], ALU.add),
                    reads=[TMP_K[pj], bAk], writes=[OUT_K[j % 4]])

                 op("sync", lambda e, osl=osl, tt=tt, ncn=ncn: e.dma_start(
                     out=y_out[tt * 128:(tt + 1) * 128, ncn * 512:(ncn + 1) * 512],
                     in_=osl), reads=[OUT_K[j % 4]], q=sST[j % 4])
             if ncn + 2 < 4:
                 emit_wout(ncn + 2)

         for i in range(4):
             emit("sync", lambda e, s_=sST[i].h, v=sST[i].n: e.wait_ge(s_, v))

         _finish_build = True

        try:
            _phases()
        except _Cut:
            op("gpsimd", lambda e: e.dma_start(out=y_out[0:128, :],
               in_=AB.ap().bitcast(F32)[:, 0:2048]), reads=[("AB",)], q=sWO)
            emit("sync", lambda e, s_=sWO.h, v=sWO.n: e.wait_ge(s_, v))

        block = ec(nc.Block())

        @block.sync
        def _(e):
            for f in ops["sync"]:
                f(e)

        @block.gpsimd
        def _(e):
            for f in ops["gpsimd"]:
                f(e)

        @block.vector
        def _(e):
            for f in ops["vector"]:
                f(e)

        @block.scalar
        def _(e):
            for f in ops["scalar"]:
                f(e)

        @block.tensor
        def _(e):
            for f in ops["tensor"]:
                f(e)

    return nc


_NC_CACHE = None


def kernel(L0_attn_x, L0_attn_W_qkv, L0_attn_W_out, L0_attn_b_out,
           L0_ffn_input2, L0_ffn_scale, L0_ffn_scale1):
    global _NC_CACHE
    if _NC_CACHE is None:
        _NC_CACHE = build_program()
    nc = _NC_CACHE

    x = np.ascontiguousarray(np.asarray(L0_attn_x, dtype=np.float32).reshape(8192, 2048))
    wqkv = np.ascontiguousarray(np.asarray(L0_attn_W_qkv, dtype=np.float32))
    wout = np.ascontiguousarray(np.asarray(L0_attn_W_out, dtype=np.float32))
    bout = np.ascontiguousarray(np.asarray(L0_attn_b_out, dtype=np.float32).reshape(1, 2048))
    in2 = np.ascontiguousarray(np.asarray(L0_ffn_input2, dtype=np.float32))
    sc = np.asarray(L0_ffn_scale, dtype=np.float32).reshape(1, 1)
    sc1 = np.asarray(L0_ffn_scale1, dtype=np.float32).reshape(1, 1)

    id32 = np.eye(128, dtype=np.float32)
    idbf = np.eye(128, dtype=ml_dtypes.bfloat16)
    maskb = np.kron(np.eye(8, dtype=np.float32),
                    np.ones((16, 16), dtype=np.float32)).astype(ml_dtypes.bfloat16)

    in_maps = []
    for c in range(8):
        in_maps.append({
            "x": x[c * T:(c + 1) * T],
            "wqkv": wqkv, "wout": wout, "bout": bout, "in2": in2,
            "sc": sc, "sc1": sc1,
            "id32": id32, "idbf": idbf, "maskb": maskb,
        })

    res = run_bass_kernel_spmd(nc, in_maps, core_ids=list(range(8)))
    y = np.concatenate([r["y"] for r in res.results], axis=0)
    return y.reshape(4, 2048, 2048)
